# version 13
# baseline (speedup 1.0000x reference)
import contextlib

import numpy as np
import ml_dtypes

import concourse.bass as bass
import concourse.mybir as mybir
from concourse.bass_utils import run_bass_kernel_spmd

F32 = mybir.dt.float32
BF16 = mybir.dt.bfloat16
I32 = mybir.dt.int32
AF = mybir.ActivationFunctionType
ALU = mybir.AluOpType
AX = mybir.AxisListType

N_CORES = 8


class Cfg:
    def __init__(self, D, INT, TOK, C):
        self.D = D
        self.INT = INT
        self.TOK = TOK
        self.C = C
        self.HF = D
        self.ND = D // 128
        self.NI = INT // 128
        self.NT = self.HF // 128
        self.NCH = TOK // C
        self.BQ = min(4, self.ND)
        self.GN = 2
        self.NGRP = self.NT // self.GN
        self.GW = self.GN * 128
        self.IB = 8
        self.KB = 8
        self.QW = min(512, self.HF)


REAL = Cfg(D=4096, INT=11008, TOK=2048, C=512)


def _merge(d, e):
    for k, v in e.items():
        if v > d.get(k, 0):
            d[k] = v


class Buf:
    __slots__ = ("w", "r")

    def __init__(self):
        self.w = {}
        self.r = {}


DMA_STREAMS = ("ct", "xw0", "xw1", "gb0", "gb1", "gb2",
               "ws0", "ws1", "ws2", "xo0", "xo1", "pf0", "pf1",
               "gbf0", "gbf1", "gbf2", "wsf0", "wsf1", "wsf2",
               "scg0", "scg1", "scg2", "scd0", "scd1", "scd2",
               "ab0", "ab1", "bw0", "bw1", "bma")


class Bld:

    def __init__(self, nc, stack, extra_streams=()):
        self.nc = nc
        self.cnt = {}
        self.snap = {}
        self.known = {e: {} for e in ("T", "V", "S", "G", "Y")}
        self.phys = {"T": "T", "V": "V", "S": "S", "G": "G", "Y": "Y", "A": "S"}
        self.eng = {"T": nc.tensor, "V": nc.vector, "S": nc.scalar,
                    "G": nc.gpsimd, "Y": nc.sync, "A": nc.scalar}
        self.sems = {}
        for s in ("T", "V", "S") + DMA_STREAMS + tuple(extra_streams):
            self.sems[s] = stack.enter_context(nc.semaphore("sm_" + s))
            self.cnt[s] = 0
            self.snap[s] = {}
        self.self_waits = 0

    def wait(self, e, deps):
        p = self.phys[e]
        eng = self.eng[e]
        items = sorted(deps.items(), key=lambda kv: kv[0] == p)
        for s, v in items:
            if v <= 0:
                continue
            if self.known[p].get(s, 0) >= v:
                continue
            eng.wait_ge(self.sems[s], v)
            if s == "T" and p == "T":
                self.self_waits += 1
            k = self.known[p]
            _merge(k, self.snap[s].get(v, {}))
            if v > k.get(s, 0):
                k[s] = v

    def emit(self, e, emit_fn, reads=(), writes=(), waw_ok=False, sem=None):
        deps = {}
        for b in reads:
            _merge(deps, b.w)
        for b in writes:
            if not waw_ok:
                _merge(deps, b.w)
                _merge(deps, b.r)
        self.wait(e, deps)
        ins = emit_fn()
        if e in ("G", "Y", "A"):
            s = sem
            a = 16
        else:
            s = e
            a = 1
        self.cnt[s] += a
        v = self.cnt[s]
        ins.then_inc(self.sems[s], a)
        snapshot = dict(self.known[self.phys[e]])
        snapshot[s] = v
        self.snap[s][v] = snapshot
        t = {s: v}
        for b in reads:
            _merge(b.r, t)
        for b in writes:
            _merge(b.w, t)
        return (s, v)

    def close_stream(self, sem, bufs):
        t = {sem: self.cnt[sem]}
        for b in bufs:
            _merge(b.w, t)


def build_nc(cfg=REAL):
    nc = bass.Bass("TRN2", target_bir_lowering=False)
    D, INT, TOK, C, HF = cfg.D, cfg.INT, cfg.TOK, cfg.C, cfg.HF
    ND, NI, NT = cfg.ND, cfg.NI, cfg.NT
    NBAT = -(-NI // cfg.IB)
    NBATK = -(-ND // cfg.KB)
    NR = -(-ND // cfg.BQ)

    xT_in = nc.dram_tensor("xT_in", [D, TOK], F32, kind="ExternalInput").ap()
    wg5 = nc.dram_tensor("wg5", [NI, 128, ND * 128], F32, kind="ExternalInput").ap()
    wu5 = nc.dram_tensor("wu5", [NI, 128, ND * 128], F32, kind="ExternalInput").ap()
    wd5 = nc.dram_tensor("wd5", [cfg.NGRP, NBAT, 128, cfg.IB * cfg.GW], F32,
                         kind="ExternalInput").ap()
    waT = nc.dram_tensor("waT", [D, HF], F32, kind="ExternalInput").ap()
    maskT = nc.dram_tensor("maskT", [D, HF], I32, kind="ExternalInput").ap()
    bias_pt = nc.dram_tensor("bias_pt", [128, NT], F32, kind="ExternalInput").ap()
    gs_in = nc.dram_tensor("gs_in", [1, 1], F32, kind="ExternalInput").ap()
    us_in = nc.dram_tensor("us_in", [1, 1], F32, kind="ExternalInput").ap()
    ds_in = nc.dram_tensor("ds_in", [1, 1], F32, kind="ExternalInput").ap()
    ones1_in = nc.dram_tensor("ones1_in", [1, 128], F32, kind="ExternalInput").ap()
    oneskb_in = nc.dram_tensor("oneskb_in", [128, 1], BF16, kind="ExternalInput").ap()
    qmat_in = nc.dram_tensor("qmat_in", [128, 4], F32, kind="ExternalInput").ap()
    outT = nc.dram_tensor("outT", [D, TOK], F32, kind="ExternalOutput").ap()
    xT_buf = nc.dram_tensor("xT_buf", [D, TOK], F32, kind="Internal").ap()
    gu_bf = nc.dram_tensor("gu_bf", [NI, 128, 2 * ND * 128], BF16, kind="Internal").ap()
    d_bf = nc.dram_tensor("d_bf", [cfg.NGRP, NBAT, 128, cfg.IB * cfg.GW], BF16,
                          kind="Internal").ap()
    a_bf = nc.dram_tensor("a_bf", [cfg.NGRP, NBATK, 128, cfg.KB * cfg.GW], BF16,
                          kind="Internal").ap()
    bm_dram = nc.dram_tensor("bm_dram", [4, ND * (HF // 32)], BF16, kind="Internal").ap()

    st = contextlib.ExitStack()
    with st:
        bld = Bld(nc, st, extra_streams=["ht%d" % r for r in range(NR)])

        def sb(name, shape, dt):
            return st.enter_context(nc.sbuf_tensor(name, shape, dt))

        HT = sb("HT", [128, ND * C], BF16)
        inter = sb("inter", [128, NI * C], BF16)
        gubuf = sb("gubuf", [128, 3 * 2 * ND * 128], BF16)
        wsb = sb("wsb", [128, 3 * cfg.IB * cfg.GW], BF16)
        xob = sb("xob", [128, 2 * cfg.GN * C], F32)
        silb = sb("silb", [128, 2 * C], BF16)
        mrow = sb("mrow", [1, C], F32)
        srow = sb("srow", [1, C], F32)
        rstd = sb("rstd", [1, C], F32)
        mb_sb = sb("mb_sb", [128, C], BF16)
        rb_sb = sb("rb_sb", [128, C], BF16)
        gsr = sb("gsr", [1, 1], F32)
        usr = sb("usr", [1, 1], F32)
        dsr = sb("dsr", [1, 1], F32)
        t11 = sb("t11", [1, 1], F32)
        gsv = sb("gsv", [128, 1], F32)
        sdv = sb("sdv", [128, 1], F32)
        epsv = sb("epsv", [128, 1], F32)
        onesb1 = sb("onesb1", [1, 128], F32)
        oneskb = sb("oneskb", [128, 1], BF16)
        qm = sb("qm", [128, 4], F32)
        biasb = sb("biasb", [128, NT], F32)
        pf = sb("pf", [128, 2 * cfg.QW], F32)
        colred = sb("colred", [128, 4 * (HF // 32)], F32)
        bmsml = sb("bmsml", [4, 2 * 128], BF16)
        EXP = sb("EXPb", [128, cfg.QW], BF16)
        mkd = sb("mkd", [128, 2 * cfg.QW], BF16)
        BMk = sb("BMk", [128, 2 * (HF // 32)], BF16)

        pb = [st.enter_context(nc.psum_tensor("pb%d" % k, [128, 512], F32))
              for k in range(8)]

        B = Buf
        interB = B()
        gu = [B(), B(), B()]
        ws = [B(), B(), B()]
        mbB, rbB = B(), B()
        xo = [B(), B()]
        sil = [B(), B()]
        sq = sil
        pbB = [B() for _ in range(8)]
        mrowB, srowB, rstdB = B(), B(), B()
        gsrB, usrB, dsrB, t11B = B(), B(), B(), B()
        gsvB, sdvB, epsvB = B(), B(), B()
        onesB, oneskB, qmB, biasB = B(), B(), B(), B()
        pfB = [B(), B()]
        bmsB = [B(), B()]
        EXPB = B()
        mkdB = [B(), B()]
        BMkB = [B(), B()]
        xin_regs = [B() for _ in range(NR)]
        xbuf_regs = [[B() for _ in range(NR)] for _ in range(cfg.NCH)]
        out_regs = [[B() for _ in range(NR)] for _ in range(cfg.NCH)]
        gbfB, dbfB, abfB, bmdB = B(), B(), B(), B()

        E = bld.emit

        E("G", lambda: nc.gpsimd.dma_start(out=onesb1[:, :], in_=ones1_in),
          writes=[onesB], sem="ct")
        E("G", lambda: nc.gpsimd.dma_start(out=oneskb[:, :], in_=oneskb_in),
          writes=[oneskB], sem="ct")
        E("G", lambda: nc.gpsimd.dma_start(out=qm[:, :], in_=qmat_in),
          writes=[qmB], sem="ct")
        E("G", lambda: nc.gpsimd.dma_start(out=biasb[:, :], in_=bias_pt),
          writes=[biasB], sem="ct")
        E("G", lambda: nc.gpsimd.dma_start(out=gsr[:, :], in_=gs_in),
          writes=[gsrB], sem="ct")
        E("G", lambda: nc.gpsimd.dma_start(out=usr[:, :], in_=us_in),
          writes=[usrB], sem="ct")
        E("G", lambda: nc.gpsimd.dma_start(out=dsr[:, :], in_=ds_in),
          writes=[dsrB], sem="ct")
        bld.close_stream("ct", [onesB, oneskB, qmB, biasB, gsrB, usrB, dsrB])
        E("V", lambda: nc.vector.memset(epsv[:, :], 1e-5), writes=[epsvB])
        E("V", lambda: nc.vector.tensor_tensor(t11[:, :], dsr[:, :], usr[:, :], ALU.mult),
          reads=[dsrB, usrB], writes=[t11B])
        E("V", lambda: nc.vector.tensor_scalar(t11[:, :], t11[:, :], 0.5, None, ALU.mult),
          writes=[t11B])
        E("T", lambda: nc.tensor.matmul(pb[4][:, 0:1], onesb1[:, :], t11[:, :],
                                        start=True, stop=True),
          reads=[onesB, t11B], writes=[pbB[4]])
        E("T", lambda: nc.tensor.matmul(pb[5][:, 0:1], onesb1[:, :], gsr[:, :],
                                        start=True, stop=True),
          reads=[onesB, gsrB], writes=[pbB[5]])
        E("V", lambda: nc.vector.tensor_copy(sdv[:, :], pb[4][:, 0:1]),
          reads=[pbB[4]], writes=[sdvB])
        E("V", lambda: nc.vector.tensor_copy(gsv[:, :], pb[5][:, 0:1]),
          reads=[pbB[5]], writes=[gsvB])

        NQ = HF // cfg.QW
        NB32 = cfg.QW // 32

        crslots = [B(), B(), B(), B()]

        def prep_bm_q(j):
            kt, q = j // NQ, j % NQ
            csl = kt % 4
            sl = j % 2
            pv = pf[:, sl * cfg.QW:(sl + 1) * cfg.QW]
            E("Y", lambda pv=pv, kt=kt, q=q: nc.sync.dma_start(
                out=pv.bitcast(I32),
                in_=maskT[kt * 128:(kt + 1) * 128, q * cfg.QW:(q + 1) * cfg.QW]),
              writes=[pfB[sl]], sem="pf%d" % sl)
            E("V", lambda pv=pv: nc.vector.tensor_copy(pv, pv.bitcast(I32)),
              reads=[pfB[sl]], writes=[pfB[sl]])
            E("V", lambda pv=pv, q=q, csl=csl: nc.vector.reduce_sum(
                colred[:, csl * (HF // 32) + q * NB32:
                       csl * (HF // 32) + (q + 1) * NB32],
                pv.rearrange("p (n r) -> p n r", r=32), axis=AX.X),
              reads=[pfB[sl]], writes=[crslots[csl]])

        def prep_bm_kt_mm(kt):
            csl = kt % 4
            E("T", lambda csl=csl: nc.tensor.matmul(
                pb[6][0:4, 0:HF // 32], qm[:, :],
                colred[:, csl * (HF // 32):(csl + 1) * (HF // 32)],
                start=True, stop=True),
              reads=[qmB, crslots[csl]], writes=[pbB[6]])
            bsl = kt % 2
            bv = bmsml[:, bsl * 128:bsl * 128 + HF // 32]
            E("V", lambda bv=bv: nc.vector.tensor_scalar(
                bv, pb[6][0:4, 0:HF // 32], 0.0, None, ALU.is_gt),
              reads=[pbB[6]], writes=[bmsB[bsl]])
            E("G", lambda bv=bv, kt=kt: nc.gpsimd.dma_start(
                out=bm_dram[0:4, kt * (HF // 32):(kt + 1) * (HF // 32)], in_=bv),
              reads=[bmsB[bsl]], writes=[bmdB], sem="bw%d" % bsl)
        NQ32 = cfg.QW // 32

        def prep_mask_q(j):
            kt, q = j // NQ, j % NQ
            ksl = kt % 2
            bmv = BMk[:, ksl * (HF // 32):(ksl + 1) * (HF // 32)]
            if q == 0:
                for g4 in range(4):
                    E("G", lambda g4=g4, kt=kt, bmv=bmv: nc.gpsimd.dma_start(
                        out=bmv[32 * g4:32 * g4 + 32, :],
                        in_=bm_dram[g4:g4 + 1, kt * (HF // 32):(kt + 1) * (HF // 32)]
                        .broadcast_to([32, HF // 32])),
                      reads=[bmdB], writes=[BMkB[ksl]], sem="bma")
            sl = j % 2
            pv = pf[:, sl * cfg.QW:(sl + 1) * cfg.QW]
            mv = mkd[:, sl * cfg.QW:(sl + 1) * cfg.QW]
            E("S", lambda q=q, bmv=bmv, ksl=ksl: nc.scalar.copy(
                EXP[:, :].rearrange("p (n r) -> p n r", r=32),
                bmv[:, q * NQ32:(q + 1) * NQ32]
                .unsqueeze(2).broadcast_to([128, NQ32, 32])),
              reads=[BMkB[ksl]], writes=[EXPB])
            E("Y", lambda pv=pv, kt=kt, q=q: nc.sync.dma_start(
                out=pv,
                in_=waT[kt * 128:(kt + 1) * 128, q * cfg.QW:(q + 1) * cfg.QW]),
              writes=[pfB[sl]], sem="pf%d" % sl)
            E("V", lambda pv=pv, mv=mv: nc.vector.tensor_tensor(
                mv, pv, EXP[:, :], ALU.mult),
              reads=[pfB[sl], EXPB], writes=[mkdB[sl]])
            bik, kk = kt // cfg.KB, kt % cfg.KB
            ng_per_q = cfg.QW // cfg.GW
            E("G", lambda mv=mv, bik=bik, kk=kk, q=q, ng_per_q=ng_per_q:
              nc.gpsimd.dma_start(
                out=a_bf[q * ng_per_q:(q + 1) * ng_per_q, bik, :,
                         kk * cfg.GW:(kk + 1) * cfg.GW]
                .rearrange("g p w -> p g w"),
                in_=mv.rearrange("p (g w) -> p g w", w=cfg.GW)),
              reads=[mkdB[sl]], writes=[abfB], sem="ab%d" % sl)

        class Hb:
            def __init__(self, view, alias):
                self.v = view
                self.buf = Buf()
                self.q4 = [Buf() for _ in range(4)]
                self.alias = alias

        htstream = [Buf() for _ in range(NR)]
        hb_HT = Hb(HT[:, :], [])
        hb_IA = Hb(inter[:, 0:ND * C], [interB]) if NI >= ND else None
        hb_IB = Hb(inter[:, ND * C:2 * ND * C], [interB]) if NI >= 2 * ND else None

        def ln_load(c, src_ap, src_regs, r, hb):
            ccols = slice(c * C, (c + 1) * C)
            d0 = r * cfg.BQ
            nb = min(cfg.BQ, ND - d0)
            E("G", lambda d0=d0, nb=nb, hb=hb: nc.gpsimd.dma_start(
                out=hb.v[:, d0 * C:(d0 + nb) * C].rearrange("p (j t) -> p j t", t=C),
                in_=src_ap[d0 * 128:(d0 + nb) * 128, ccols]
                .rearrange("(j p) t -> p j t", p=128)),
              reads=[src_regs[r]], writes=[hb.buf, htstream[r]] + hb.alias,
              sem="ht%d" % r)

        def ln_stats_region(r, hb):
            d0 = r * cfg.BQ
            nb = min(cfg.BQ, ND - d0)
            for dt in range(d0, d0 + nb):
                sl = dt % 2
                sv = silb[:, sl * C:(sl + 1) * C]
                hv = hb.v[:, dt * C:(dt + 1) * C]
                E("V", lambda sv=sv, hv=hv: nc.vector.tensor_tensor(sv, hv, hv, ALU.mult),
                  reads=[hb.buf], writes=[sq[sl]])
                E("T", lambda hv=hv, dt=dt: nc.tensor.matmul(
                    pb[2][0:1, 0:C], oneskb[:, :], hv,
                    start=(dt == 0), stop=(dt == ND - 1)),
                  reads=[oneskB, hb.buf], writes=[pbB[2]], waw_ok=(dt > 0))
                E("T", lambda sv=sv, dt=dt: nc.tensor.matmul(
                    pb[3][0:1, 0:C], oneskb[:, :], sv,
                    start=(dt == 0), stop=(dt == ND - 1)),
                  reads=[oneskB, sq[sl]], writes=[pbB[3]], waw_ok=(dt > 0))

        def ln_finalize():
            inv_d = 1.0 / D
            E("V", lambda: nc.vector.tensor_scalar(mrow[:, :], pb[2][0:1, 0:C],
                                                   inv_d, None, ALU.mult),
              reads=[pbB[2]], writes=[mrowB])
            E("T", lambda: nc.tensor.matmul(pb[2][:, 0:C], onesb1[:, :], mrow[:, :],
                                            start=True, stop=True),
              reads=[onesB, mrowB], writes=[pbB[2]])
            E("S", lambda: nc.scalar.copy(mb_sb[:, :], pb[2][:, 0:C]),
              reads=[pbB[2]], writes=[mbB])
            E("V", lambda: nc.vector.tensor_scalar(srow[:, :], pb[3][0:1, 0:C],
                                                   inv_d, None, ALU.mult),
              reads=[pbB[3]], writes=[srowB])
            E("V", lambda: nc.vector.tensor_tensor(rstd[:, :], mrow[:, :], mrow[:, :],
                                                   ALU.mult),
              reads=[mrowB], writes=[rstdB])
            E("V", lambda: nc.vector.tensor_tensor(srow[:, :], srow[:, :], rstd[:, :],
                                                   ALU.subtract),
              reads=[srowB, rstdB], writes=[srowB])
            E("S", lambda: nc.scalar.activation(rstd[:, :], srow[:, :], AF.Sqrt,
                                                bias=epsv[0:1, 0:1]),
              reads=[srowB, epsvB], writes=[rstdB])
            E("V", lambda: nc.vector.reciprocal(rstd[:, :], rstd[:, :]),
              reads=[rstdB], writes=[rstdB])
            E("T", lambda: nc.tensor.matmul(pb[3][:, 0:C], onesb1[:, :], rstd[:, :],
                                            start=True, stop=True),
              reads=[onesB, rstdB], writes=[pbB[3]])
            E("S", lambda: nc.scalar.copy(rb_sb[:, :], pb[3][:, 0:C]),
              reads=[pbB[3]], writes=[rbB])

        def ln_affine(hb):
            for dt in range(ND):
                hv = hb.v[:, dt * C:(dt + 1) * C]
                E("V", lambda hv=hv: nc.vector.tensor_tensor(hv, hv, mb_sb[:, :],
                                                             ALU.subtract),
                  reads=[mbB, hb.buf], writes=[hb.buf] + hb.alias)
            for dt in range(ND):
                hv = hb.v[:, dt * C:(dt + 1) * C]
                E("V", lambda hv=hv: nc.vector.tensor_tensor(hv, hv, rb_sb[:, :],
                                                             ALU.mult),
                  reads=[rbB, hb.buf],
                  writes=[hb.buf, hb.q4[dt // max(1, ND // 4)]] + hb.alias)

        def ln_full(c, src_ap, src_regs, hb):
            for r in range(NR):
                ln_load(c, src_ap, src_regs, r, hb)
            for r in range(NR):
                ln_stats_region(r, hb)
            ln_finalize()
            ln_affine(hb)

        def emit_ffn(c, src_ap, src_regs, dst_ap, dst_regs, first, hooks, hb,
                     a_hook=None):
            ccols = slice(c * C, (c + 1) * C)
            GUW = 2 * ND * 128
            for i in range(NI):
                sl = i % 3
                guv = gubuf[:, sl * GUW:(sl + 1) * GUW]
                gv = guv[:, 0:ND * 128]
                uv = guv[:, ND * 128:GUW]
                if first:
                    E("G", lambda gv=gv, i=i: nc.gpsimd.dma_start(
                        out=gv, in_=wg5[i]),
                      writes=[gu[sl]], sem="gbf%d" % sl)
                    E("G", lambda uv=uv, i=i: nc.gpsimd.dma_start(
                        out=uv, in_=wu5[i]),
                      writes=[gu[sl]], waw_ok=True, sem="gbf%d" % sl)
                    E("A", lambda guv=guv, i=i: nc.scalar.dma_start(
                        out=gu_bf[i], in_=guv),
                      reads=[gu[sl]], writes=[gbfB], sem="scg%d" % sl)
                else:
                    E("Y", lambda guv=guv, i=i: nc.sync.dma_start(
                        out=guv, in_=gu_bf[i]),
                      reads=[gbfB], writes=[gu[sl]], sem="gb%d" % sl)
                bG = pbB[0] if i % 2 == 0 else pbB[2]
                bU = pbB[1] if i % 2 == 0 else pbB[3]
                pG = pb[0] if i % 2 == 0 else pb[2]
                pU = pb[1] if i % 2 == 0 else pb[3]

                def mmchain(pX, wv, d0, d1):
                    ins = None
                    for dt in range(d0, d1):
                        ins = nc.tensor.matmul(
                            pX[:, 0:C], wv[:, dt * 128:(dt + 1) * 128],
                            hb.v[:, dt * C:(dt + 1) * C],
                            start=(dt == 0), stop=(dt == ND - 1))
                    return ins
                if i < 2 and ND >= 4:
                    qn = ND // 4
                    for j in range(4):
                        E("T", lambda pG=pG, gv=gv, j=j, qn=qn:
                          mmchain(pG, gv, j * qn, (j + 1) * qn),
                          reads=[gu[sl], hb.q4[j]], writes=[bG], waw_ok=(j > 0))
                    for j in range(4):
                        E("T", lambda pU=pU, uv=uv, j=j, qn=qn:
                          mmchain(pU, uv, j * qn, (j + 1) * qn),
                          reads=[gu[sl], hb.q4[j]], writes=[bU], waw_ok=(j > 0))
                else:
                    E("T", lambda pG=pG, gv=gv: mmchain(pG, gv, 0, ND),
                      reads=[gu[sl], hb.buf], writes=[bG])
                    E("T", lambda pU=pU, uv=uv: mmchain(pU, uv, 0, ND),
                      reads=[gu[sl], hb.buf], writes=[bU])
                slp = i % 2
                sv = silb[:, slp * C:(slp + 1) * C]
                E("S", lambda sv=sv, pG=pG: nc.scalar.activation(
                    sv, pG[:, 0:C], AF.Silu, scale=gsv[:, 0:1]),
                  reads=[bG, gsvB], writes=[sil[slp]])
                E("V", lambda sv=sv, pU=pU, i=i: nc.vector.tensor_tensor(
                    inter[:, i * C:(i + 1) * C], sv, pU[:, 0:C], ALU.mult),
                  reads=[sil[slp], bU], writes=[interB])
                if a_hook:
                    a_hook(i)
            if a_hook:
                a_hook(None)
            rot = [(4, 5), (6, 7), (0, 1)]

            def wd_slot(g, bi):
                i0 = bi * cfg.IB
                nb = min(cfg.IB, NI - i0)
                wsl = (g * NBAT + bi) % 3
                wv = wsb[:, wsl * cfg.IB * cfg.GW:wsl * cfg.IB * cfg.GW + nb * cfg.GW]
                return i0, nb, wsl, wv

            def wd_load(g, bi):
                i0, nb, wsl, wv = wd_slot(g, bi)
                if first:
                    E("G", lambda wv=wv, nb=nb, g=g, bi=bi: nc.gpsimd.dma_start(
                        out=wv, in_=wd5[g, bi, :, 0:nb * cfg.GW]),
                      writes=[ws[wsl]], sem="wsf%d" % wsl)
                    E("A", lambda wv=wv, nb=nb, g=g, bi=bi: nc.scalar.dma_start(
                        out=d_bf[g, bi, :, 0:nb * cfg.GW], in_=wv),
                      reads=[ws[wsl]], writes=[dbfB], sem="scd%d" % wsl)
                else:
                    E("Y", lambda wv=wv, nb=nb, g=g, bi=bi: nc.sync.dma_start(
                        out=wv, in_=d_bf[g, bi, :, 0:nb * cfg.GW]),
                      reads=[dbfB], writes=[ws[wsl]], sem="ws%d" % wsl)

            pairs = [(g, bi) for g in range(cfg.NGRP) for bi in range(NBAT)]
            cursor = [0]

            def wd_prefetch(upto):
                while cursor[0] < min(upto, len(pairs)):
                    wd_load(*pairs[cursor[0]])
                    cursor[0] += 1

            kidx = 0
            for g in range(cfg.NGRP):
                k0i, k1i = rot[g % 3]
                xsl = g % 2
                xv = xob[:, xsl * cfg.GN * C:(xsl + 1) * cfg.GN * C]
                E("A", lambda xv=xv, g=g: nc.scalar.dma_start(
                    out=xv.rearrange("p (j t) -> p j t", t=C),
                    in_=src_ap[g * cfg.GW:(g + 1) * cfg.GW, ccols]
                    .rearrange("(j p) t -> p j t", p=128)),
                  reads=[src_regs[g * cfg.GN // cfg.BQ]], writes=[xo[xsl]],
                  sem="xo%d" % xsl)
                for bi in range(NBAT):
                    i0, nb, wsl, wv = wd_slot(g, bi)
                    wv3 = wv.rearrange("p (b w) -> p b w", w=cfg.GW)
                    wd_prefetch(kidx + 2)
                    kidx += 1

                    def bchain(wv3=wv3, i0=i0, nb=nb, k0i=k0i, k1i=k1i):
                        ins = None
                        for ii in range(nb):
                            i = i0 + ii
                            ins = nc.tensor.matmul(
                                pb[k0i][:, 0:C], wv3[:, ii, 0:128],
                                inter[:, i * C:(i + 1) * C],
                                start=(i == 0), stop=(i == NI - 1))
                            ins = nc.tensor.matmul(
                                pb[k1i][:, 0:C], wv3[:, ii, 128:256],
                                inter[:, i * C:(i + 1) * C],
                                start=(i == 0), stop=(i == NI - 1))
                        return ins
                    E("T", bchain, reads=[ws[wsl], interB],
                      writes=[pbB[k0i], pbB[k1i]], waw_ok=(bi > 0))
                for nt in range(cfg.GN):
                    ki = (k0i, k1i)[nt]
                    xvs = xv[:, nt * C:(nt + 1) * C]
                    E("V", lambda xvs=xvs, ki=ki: nc.vector.scalar_tensor_tensor(
                        xvs, pb[ki][:, 0:C], sdv[:, 0:1], xvs, ALU.mult, ALU.add),
                      reads=[pbB[ki], sdvB, xo[xsl]], writes=[xo[xsl]])
                E("A", lambda xv=xv, g=g: nc.scalar.dma_start(
                    out=dst_ap[g * cfg.GW:(g + 1) * cfg.GW, ccols]
                    .rearrange("(j p) t -> p j t", p=128),
                    in_=xv.rearrange("p (j t) -> p j t", t=C)),
                  reads=[xo[xsl]], writes=[dst_regs[g * cfg.GN // cfg.BQ]],
                  sem="xw%d" % xsl)
                hk = hooks.get(g)
                if hk:
                    hk()

        def emit_attn(c, src_ap, src_regs, dst_ap, dst_regs, hooks, hb):
            ccols = slice(c * C, (c + 1) * C)
            rot = [(4, 5), (6, 7), (0, 1)]

            def wa_slot(g, bi):
                k0 = bi * cfg.KB
                nb = min(cfg.KB, ND - k0)
                wsl = (g * NBATK + bi) % 3
                wv = wsb[:, wsl * cfg.IB * cfg.GW:wsl * cfg.IB * cfg.GW + nb * cfg.GW]
                return k0, nb, wsl, wv

            def wa_load(g, bi):
                k0, nb, wsl, wv = wa_slot(g, bi)
                E("Y", lambda wv=wv, nb=nb, g=g, bi=bi: nc.sync.dma_start(
                    out=wv, in_=a_bf[g, bi, :, 0:nb * cfg.GW]),
                  reads=[abfB], writes=[ws[wsl]], sem="ws%d" % wsl)

            pairs = [(g, bi) for g in range(cfg.NGRP) for bi in range(NBATK)]
            cursor = [0]

            def wa_prefetch(upto):
                while cursor[0] < min(upto, len(pairs)):
                    wa_load(*pairs[cursor[0]])
                    cursor[0] += 1

            kidx = 0
            for g in range(cfg.NGRP):
                k0i, k1i = rot[g % 3]
                xsl = g % 2
                xv = xob[:, xsl * cfg.GN * C:(xsl + 1) * cfg.GN * C]
                E("A", lambda xv=xv, g=g: nc.scalar.dma_start(
                    out=xv.rearrange("p (j t) -> p j t", t=C),
                    in_=src_ap[g * cfg.GW:(g + 1) * cfg.GW, ccols]
                    .rearrange("(j p) t -> p j t", p=128)),
                  reads=[src_regs[g * cfg.GN // cfg.BQ]], writes=[xo[xsl]],
                  sem="xo%d" % xsl)
                for bi in range(NBATK):
                    k0, nb, wsl, wv = wa_slot(g, bi)
                    wv3 = wv.rearrange("p (b w) -> p b w", w=cfg.GW)
                    wa_prefetch(kidx + 2)
                    kidx += 1

                    def achain(wv3=wv3, k0=k0, nb=nb, k0i=k0i, k1i=k1i):
                        ins = None
                        for kk in range(nb):
                            kt = k0 + kk
                            ins = nc.tensor.matmul(
                                pb[k0i][:, 0:C], wv3[:, kk, 0:128],
                                hb.v[:, kt * C:(kt + 1) * C],
                                start=(kt == 0), stop=(kt == ND - 1))
                            ins = nc.tensor.matmul(
                                pb[k1i][:, 0:C], wv3[:, kk, 128:256],
                                hb.v[:, kt * C:(kt + 1) * C],
                                start=(kt == 0), stop=(kt == ND - 1))
                        return ins
                    hq = hb.q4[min(3, ((bi + 1) * cfg.KB - 1) // max(1, ND // 4))] \
                        if g == 0 and ND >= 4 else hb.buf
                    E("T", achain, reads=[ws[wsl], hq],
                      writes=[pbB[k0i], pbB[k1i]], waw_ok=(bi > 0))
                for nt in range(cfg.GN):
                    ki = (k0i, k1i)[nt]
                    gnt = g * cfg.GN + nt
                    xvs = xv[:, nt * C:(nt + 1) * C]
                    E("V", lambda xvs=xvs, ki=ki, gnt=gnt: nc.vector.scalar_tensor_tensor(
                        xvs, pb[ki][:, 0:C], biasb[:, gnt:gnt + 1], xvs,
                        ALU.add, ALU.add),
                      reads=[pbB[ki], biasB, xo[xsl]], writes=[xo[xsl]])
                E("A", lambda xv=xv, g=g: nc.scalar.dma_start(
                    out=dst_ap[g * cfg.GW:(g + 1) * cfg.GW, ccols]
                    .rearrange("(j p) t -> p j t", p=128),
                    in_=xv.rearrange("p (j t) -> p j t", t=C)),
                  reads=[xo[xsl]], writes=[dst_regs[g * cfg.GN // cfg.BQ]],
                  sem="xw%d" % xsl)
                hk = hooks.get(g)
                if hk:
                    hk()

        jobs = []
        for c in range(cfg.NCH):
            jobs.append(("ffn", c, xT_in, xin_regs, xT_buf, xbuf_regs[c], c == 0))
        for c in range(cfg.NCH):
            jobs.append(("attn", c, xT_buf, xbuf_regs[c], xT_buf, xbuf_regs[c], False))
        for c in range(cfg.NCH):
            jobs.append(("ffn", c, xT_buf, xbuf_regs[c], outT, out_regs[c], False))

        ln_done = [False] * (len(jobs) + 1)

        h_home = []
        a_seen = 0
        for jb in jobs:
            if jb[0] == "ffn":
                h_home.append(hb_HT)
            else:
                if a_seen == 0 or hb_IA is None:
                    h_home.append(hb_HT)
                elif a_seen % 2 == 1 or hb_IB is None:
                    h_home.append(hb_IA)
                else:
                    h_home.append(hb_IB)
                a_seen += 1

        def make_hooks(k):
            if k + 1 >= len(jobs):
                return {}
            n_kind, n_c, n_src, n_sregs = jobs[k + 1][:4]
            hbn = h_home[k + 1]
            state = {"loads": 0, "stats": 0}

            def hook(g):
                def run():
                    if g == 0:
                        while state["loads"] < NR:
                            ln_load(n_c, n_src, n_sregs, state["loads"], hbn)
                            state["loads"] += 1
                    limit = NR if g >= min(2, cfg.NGRP - 1) else 0
                    while state["stats"] < limit:
                        ln_stats_region(state["stats"], hbn)
                        state["stats"] += 1
                    if g == cfg.NGRP - 1:
                        ln_finalize()
                        ln_affine(hbn)
                        ln_done[k + 1] = True
                return run
            return {g: hook(g) for g in range(cfg.NGRP)}

        NQT = ND * NQ
        last_prep = cfg.NCH - 1
        n_hooks = max(1, (last_prep) * NI)
        QPH = max(1, -(-2 * NQT // n_hooks))
        pstate = {"bmq": 0, "bm2": 0, "maskq": 0}

        def prep_piece():
            if pstate["bmq"] < NQT:
                prep_bm_q(pstate["bmq"]); pstate["bmq"] += 1
                while pstate["bm2"] < ND and \
                        (pstate["bm2"] + 1) * NQ + 2 <= pstate["bmq"]:
                    prep_bm_kt_mm(pstate["bm2"]); pstate["bm2"] += 1
                return True
            if pstate["bm2"] < ND:
                prep_bm_kt_mm(pstate["bm2"]); pstate["bm2"] += 1
                return True
            if pstate["maskq"] < NQT:
                prep_mask_q(pstate["maskq"]); pstate["maskq"] += 1
                return True
            return False

        def prep_hook_for(k):
            if not (1 <= k <= last_prep):
                return None
            final = k == last_prep

            def hook(i):
                if i is None:
                    if final:
                        while prep_piece():
                            pass
                    return
                for _ in range(QPH):
                    if not prep_piece():
                        break
            return hook

        for k, jb in enumerate(jobs):
            kind, c, s_ap, s_regs, d_ap, d_regs, first = jb
            if not ln_done[k]:
                ln_full(c, s_ap, s_regs, h_home[k])
                ln_done[k] = True
            if kind == "ffn":
                emit_ffn(c, s_ap, s_regs, d_ap, d_regs, first, make_hooks(k),
                         h_home[k], a_hook=prep_hook_for(k))
            else:
                emit_attn(c, s_ap, s_regs, d_ap, d_regs, make_hooks(k), h_home[k])

        gq = ["ct", "ab0", "ab1", "bw0", "bw1", "bma",
              "gbf0", "gbf1", "gbf2", "wsf0", "wsf1", "wsf2"]
        gq += ["ht%d" % r for r in range(NR)]
        for s in gq:
            nc.gpsimd.wait_ge(bld.sems[s], bld.cnt[s])
        yq = ["pf0", "pf1", "gb0", "gb1", "gb2", "ws0", "ws1", "ws2"]
        for s in yq:
            nc.sync.wait_ge(bld.sems[s], bld.cnt[s])
        for s in ("scg0", "scg1", "scg2", "scd0", "scd1", "scd2",
                  "xo0", "xo1", "xw0", "xw1"):
            nc.scalar.wait_ge(bld.sems[s], bld.cnt[s])
        for s in bld.cnt:
            assert bld.cnt[s] < 2 ** 31, (s, bld.cnt[s])
    return nc


def _tile_gate(cfg, w):
    return np.ascontiguousarray(
        w.reshape(cfg.NI, 128, cfg.ND, 128).transpose(0, 3, 2, 1)
        .reshape(cfg.NI, 128, cfg.ND * 128))


def _tile_down(cfg, wd):
    nbat = -(-cfg.NI // cfg.IB)
    pad = nbat * cfg.IB * 128
    wdT = wd.T
    buf = np.zeros((pad, cfg.D), dtype=wd.dtype)
    buf[:cfg.INT] = wdT
    return np.ascontiguousarray(
        buf.reshape(nbat, cfg.IB, 128, cfg.NGRP, cfg.GW)
        .transpose(3, 0, 2, 1, 4).reshape(cfg.NGRP, nbat, 128, cfg.IB * cfg.GW))


def _shared_inputs(cfg, wg, wu, wd, wa, mask, bias, gs, us, ds):
    f32 = np.float32
    qmat = np.zeros((128, 4), dtype=f32)
    for r in range(128):
        qmat[r, r // 32] = 1.0
    return dict(
        wg5=_tile_gate(cfg, np.asarray(wg, dtype=f32)),
        wu5=_tile_gate(cfg, np.asarray(wu, dtype=f32)),
        wd5=_tile_down(cfg, np.asarray(wd, dtype=f32)),
        waT=np.ascontiguousarray(np.asarray(wa, dtype=f32).T),
        maskT=np.ascontiguousarray(np.asarray(mask, dtype=np.int32).T),
        bias_pt=np.ascontiguousarray(np.asarray(bias, dtype=f32).reshape(cfg.NT, 128).T),
        gs_in=np.asarray(gs, dtype=f32).reshape(1, 1),
        us_in=np.asarray(us, dtype=f32).reshape(1, 1),
        ds_in=np.asarray(ds, dtype=f32).reshape(1, 1),
        ones1_in=np.ones((1, 128), dtype=f32),
        oneskb_in=np.ones((128, 1), dtype=ml_dtypes.bfloat16),
        qmat_in=qmat,
    )


def _host_layout(cfg, inputs):
    data = np.asarray(inputs["L0_attn_data"], dtype=np.float32)
    X = np.ascontiguousarray(data.reshape(-1, cfg.D).T)
    shared = _shared_inputs(
        cfg, inputs["L0_ffn_gate_w"], inputs["L0_ffn_up_w"],
        inputs["L0_ffn_down_w"], inputs["L0_attn_weight"], inputs["L0_attn_mask"],
        inputs["L0_attn_bias"], inputs["L0_ffn_gate_scale"],
        inputs["L0_ffn_up_scale"], inputs["L0_ffn_down_scale"])
    in_maps = []
    for j in range(N_CORES):
        m = dict(shared)
        m["xT_in"] = np.ascontiguousarray(X[:, j * cfg.TOK:(j + 1) * cfg.TOK])
        in_maps.append(m)
    return in_maps


def kernel(**inputs):
    cfg = REAL
    in_maps = _host_layout(cfg, inputs)
    nc = build_nc(cfg)
    res = run_bass_kernel_spmd(nc, in_maps, core_ids=list(range(N_CORES)))
    shards = [np.asarray(r["outT"]) for r in res.results]
    full = np.concatenate(shards, axis=1)
    out = np.ascontiguousarray(full.T).reshape(np.asarray(inputs["L0_attn_data"]).shape)
    return out.astype(np.float32, copy=False)


# version 14 (final)
# speedup vs baseline: 1.0710x; 1.0710x over previous
import contextlib

import numpy as np
import ml_dtypes

import concourse.bass as bass
import concourse.mybir as mybir
from concourse.bass_utils import run_bass_kernel_spmd

F32 = mybir.dt.float32
BF16 = mybir.dt.bfloat16
I32 = mybir.dt.int32
AF = mybir.ActivationFunctionType
ALU = mybir.AluOpType
AX = mybir.AxisListType

N_CORES = 8


class Cfg:
    def __init__(self, D, INT, TOK, C):
        self.D = D
        self.INT = INT
        self.TOK = TOK
        self.C = C
        self.HF = D
        self.ND = D // 128
        self.NI = INT // 128
        self.NT = self.HF // 128
        self.NCH = TOK // C
        self.BQ = min(4, self.ND)
        self.GN = 2
        self.NGRP = self.NT // self.GN
        self.GW = self.GN * 128
        self.IB = 8
        self.KB = 8
        self.QW = min(512, self.HF)


REAL = Cfg(D=4096, INT=11008, TOK=2048, C=512)


def _merge(d, e):
    for k, v in e.items():
        if v > d.get(k, 0):
            d[k] = v


class Buf:
    __slots__ = ("w", "r")

    def __init__(self):
        self.w = {}
        self.r = {}


DMA_STREAMS = ("ct", "xw0", "xw1", "gb0", "gb1", "gb2",
               "ws0", "ws1", "ws2", "xo0", "xo1", "pf0", "pf1",
               "gbf0", "gbf1", "gbf2", "wsf0", "wsf1", "wsf2",
               "scg0", "scg1", "scg2", "scd0", "scd1", "scd2",
               "ab0", "ab1", "bw0", "bw1", "bma")


class Bld:

    def __init__(self, nc, stack, extra_streams=()):
        self.nc = nc
        self.cnt = {}
        self.snap = {}
        self.known = {e: {} for e in ("T", "V", "S", "G", "Y")}
        self.phys = {"T": "T", "V": "V", "S": "S", "G": "G", "Y": "Y", "A": "S"}
        self.eng = {"T": nc.tensor, "V": nc.vector, "S": nc.scalar,
                    "G": nc.gpsimd, "Y": nc.sync, "A": nc.scalar}
        self.sems = {}
        for s in ("T", "V", "S") + DMA_STREAMS + tuple(extra_streams):
            self.sems[s] = stack.enter_context(nc.semaphore("sm_" + s))
            self.cnt[s] = 0
            self.snap[s] = {}
        self.self_waits = 0

    def wait(self, e, deps):
        p = self.phys[e]
        eng = self.eng[e]
        items = sorted(deps.items(), key=lambda kv: kv[0] == p)
        for s, v in items:
            if v <= 0:
                continue
            if self.known[p].get(s, 0) >= v:
                continue
            eng.wait_ge(self.sems[s], v)
            if s == "T" and p == "T":
                self.self_waits += 1
            k = self.known[p]
            _merge(k, self.snap[s].get(v, {}))
            if v > k.get(s, 0):
                k[s] = v

    def emit(self, e, emit_fn, reads=(), writes=(), waw_ok=False, sem=None):
        deps = {}
        for b in reads:
            _merge(deps, b.w)
        for b in writes:
            if not waw_ok:
                _merge(deps, b.w)
                _merge(deps, b.r)
        self.wait(e, deps)
        ins = emit_fn()
        if e in ("G", "Y", "A"):
            s = sem
            a = 16
        else:
            s = e
            a = 1
        self.cnt[s] += a
        v = self.cnt[s]
        ins.then_inc(self.sems[s], a)
        snapshot = dict(self.known[self.phys[e]])
        snapshot[s] = v
        self.snap[s][v] = snapshot
        t = {s: v}
        for b in reads:
            _merge(b.r, t)
        for b in writes:
            _merge(b.w, t)
        return (s, v)

    def close_stream(self, sem, bufs):
        t = {sem: self.cnt[sem]}
        for b in bufs:
            _merge(b.w, t)


def build_nc(cfg=REAL):
    nc = bass.Bass("TRN2", target_bir_lowering=False)
    D, INT, TOK, C, HF = cfg.D, cfg.INT, cfg.TOK, cfg.C, cfg.HF
    ND, NI, NT = cfg.ND, cfg.NI, cfg.NT
    NBAT = -(-NI // cfg.IB)
    NBATK = -(-ND // cfg.KB)
    NR = -(-ND // cfg.BQ)

    xT_in = nc.dram_tensor("xT_in", [D, TOK], F32, kind="ExternalInput").ap()
    wg5 = nc.dram_tensor("wg5", [NI, 128, ND * 128], F32, kind="ExternalInput").ap()
    wu5 = nc.dram_tensor("wu5", [NI, 128, ND * 128], F32, kind="ExternalInput").ap()
    wd5 = nc.dram_tensor("wd5", [cfg.NGRP, NBAT, 128, cfg.IB * cfg.GW], F32,
                         kind="ExternalInput").ap()
    waT = nc.dram_tensor("waT", [D, HF], F32, kind="ExternalInput").ap()
    maskT = nc.dram_tensor("maskT", [D, HF], I32, kind="ExternalInput").ap()
    bias_pt = nc.dram_tensor("bias_pt", [128, NT], F32, kind="ExternalInput").ap()
    gs_in = nc.dram_tensor("gs_in", [1, 1], F32, kind="ExternalInput").ap()
    us_in = nc.dram_tensor("us_in", [1, 1], F32, kind="ExternalInput").ap()
    ds_in = nc.dram_tensor("ds_in", [1, 1], F32, kind="ExternalInput").ap()
    ones1_in = nc.dram_tensor("ones1_in", [1, 128], F32, kind="ExternalInput").ap()
    oneskb_in = nc.dram_tensor("oneskb_in", [128, 1], BF16, kind="ExternalInput").ap()
    qmat_in = nc.dram_tensor("qmat_in", [128, 4], F32, kind="ExternalInput").ap()
    outT = nc.dram_tensor("outT", [D, TOK], F32, kind="ExternalOutput").ap()
    xT_buf = nc.dram_tensor("xT_buf", [D, TOK], F32, kind="Internal").ap()
    gu_bf = nc.dram_tensor("gu_bf", [NI, 128, 2 * ND * 128], BF16, kind="Internal").ap()
    d_bf = nc.dram_tensor("d_bf", [cfg.NGRP, NBAT, 128, cfg.IB * cfg.GW], BF16,
                          kind="Internal").ap()
    a_bf = nc.dram_tensor("a_bf", [cfg.NGRP, NBATK, 128, cfg.KB * cfg.GW], BF16,
                          kind="Internal").ap()
    bm_dram = nc.dram_tensor("bm_dram", [4, ND * (HF // 32)], BF16, kind="Internal").ap()

    st = contextlib.ExitStack()
    with st:
        bld = Bld(nc, st, extra_streams=["ht%d" % r for r in range(NR)])

        def sb(name, shape, dt):
            return st.enter_context(nc.sbuf_tensor(name, shape, dt))

        HT = sb("HT", [128, ND * C], BF16)
        inter = sb("inter", [128, NI * C], BF16)
        gubuf = sb("gubuf", [128, 3 * 2 * ND * 128], BF16)
        wsb = sb("wsb", [128, 3 * cfg.IB * cfg.GW], BF16)
        xob = sb("xob", [128, 2 * cfg.GN * C], F32)
        silb = sb("silb", [128, 2 * C], BF16)
        mrow = sb("mrow", [1, C], F32)
        srow = sb("srow", [1, C], F32)
        rstd = sb("rstd", [1, C], F32)
        mb_sb = sb("mb_sb", [128, C], BF16)
        rb_sb = sb("rb_sb", [128, C], BF16)
        gsr = sb("gsr", [1, 1], F32)
        usr = sb("usr", [1, 1], F32)
        dsr = sb("dsr", [1, 1], F32)
        t11 = sb("t11", [1, 1], F32)
        gsv = sb("gsv", [128, 1], F32)
        sdv = sb("sdv", [128, 1], F32)
        epsv = sb("epsv", [128, 1], F32)
        onesb1 = sb("onesb1", [1, 128], F32)
        oneskb = sb("oneskb", [128, 1], BF16)
        qm = sb("qm", [128, 4], F32)
        biasb = sb("biasb", [128, NT], F32)
        pf = sb("pf", [128, 2 * cfg.QW], F32)
        colred = sb("colred", [128, 4 * (HF // 32)], F32)
        bmsml = sb("bmsml", [4, 2 * 128], BF16)
        EXP = sb("EXPb", [128, cfg.QW], BF16)
        mkd = sb("mkd", [128, 2 * cfg.QW], BF16)
        BMk = sb("BMk", [128, 2 * (HF // 32)], BF16)

        pb = [st.enter_context(nc.psum_tensor("pb%d" % k, [128, 512], F32))
              for k in range(8)]

        B = Buf
        interB = B()
        gu = [B(), B(), B()]
        ws = [B(), B(), B()]
        mbB, rbB = B(), B()
        xo = [B(), B()]
        sil = [B(), B()]
        sq = sil
        pbB = [B() for _ in range(8)]
        mrowB, srowB, rstdB = B(), B(), B()
        gsrB, usrB, dsrB, t11B = B(), B(), B(), B()
        gsvB, sdvB, epsvB = B(), B(), B()
        onesB, oneskB, qmB, biasB = B(), B(), B(), B()
        pfB = [B(), B()]
        bmsB = [B(), B()]
        EXPB = B()
        mkdB = [B(), B()]
        BMkB = [B(), B()]
        xin_regs = [B() for _ in range(NR)]
        xbuf_regs = [[B() for _ in range(NR)] for _ in range(cfg.NCH)]
        out_regs = [[B() for _ in range(NR)] for _ in range(cfg.NCH)]
        gbfB, dbfB, abfB, bmdB = B(), B(), B(), B()

        E = bld.emit

        E("G", lambda: nc.gpsimd.dma_start(out=onesb1[:, :], in_=ones1_in),
          writes=[onesB], sem="ct")
        E("G", lambda: nc.gpsimd.dma_start(out=oneskb[:, :], in_=oneskb_in),
          writes=[oneskB], sem="ct")
        E("G", lambda: nc.gpsimd.dma_start(out=qm[:, :], in_=qmat_in),
          writes=[qmB], sem="ct")
        E("G", lambda: nc.gpsimd.dma_start(out=biasb[:, :], in_=bias_pt),
          writes=[biasB], sem="ct")
        E("G", lambda: nc.gpsimd.dma_start(out=gsr[:, :], in_=gs_in),
          writes=[gsrB], sem="ct")
        E("G", lambda: nc.gpsimd.dma_start(out=usr[:, :], in_=us_in),
          writes=[usrB], sem="ct")
        E("G", lambda: nc.gpsimd.dma_start(out=dsr[:, :], in_=ds_in),
          writes=[dsrB], sem="ct")
        bld.close_stream("ct", [onesB, oneskB, qmB, biasB, gsrB, usrB, dsrB])
        E("V", lambda: nc.vector.memset(epsv[:, :], 1e-5), writes=[epsvB])
        E("V", lambda: nc.vector.tensor_tensor(t11[:, :], dsr[:, :], usr[:, :], ALU.mult),
          reads=[dsrB, usrB], writes=[t11B])
        E("V", lambda: nc.vector.tensor_scalar(t11[:, :], t11[:, :], 0.5, None, ALU.mult),
          writes=[t11B])
        E("T", lambda: nc.tensor.matmul(pb[4][:, 0:1], onesb1[:, :], t11[:, :],
                                        start=True, stop=True),
          reads=[onesB, t11B], writes=[pbB[4]])
        E("T", lambda: nc.tensor.matmul(pb[5][:, 0:1], onesb1[:, :], gsr[:, :],
                                        start=True, stop=True),
          reads=[onesB, gsrB], writes=[pbB[5]])
        E("V", lambda: nc.vector.tensor_copy(sdv[:, :], pb[4][:, 0:1]),
          reads=[pbB[4]], writes=[sdvB])
        E("V", lambda: nc.vector.tensor_copy(gsv[:, :], pb[5][:, 0:1]),
          reads=[pbB[5]], writes=[gsvB])

        NQ = HF // cfg.QW
        NB32 = cfg.QW // 32

        crslots = [B(), B(), B(), B()]

        def prep_bm_q(j):
            kt, q = j // NQ, j % NQ
            csl = kt % 4
            sl = j % 2
            pv = pf[:, sl * cfg.QW:(sl + 1) * cfg.QW]
            E("G", lambda pv=pv, kt=kt, q=q: nc.gpsimd.dma_start(
                out=pv.bitcast(I32),
                in_=maskT[kt * 128:(kt + 1) * 128, q * cfg.QW:(q + 1) * cfg.QW]),
              writes=[pfB[sl]], sem="pf%d" % sl)
            E("V", lambda pv=pv: nc.vector.tensor_copy(pv, pv.bitcast(I32)),
              reads=[pfB[sl]], writes=[pfB[sl]])
            E("V", lambda pv=pv, q=q, csl=csl: nc.vector.reduce_sum(
                colred[:, csl * (HF // 32) + q * NB32:
                       csl * (HF // 32) + (q + 1) * NB32],
                pv.rearrange("p (n r) -> p n r", r=32), axis=AX.X),
              reads=[pfB[sl]], writes=[crslots[csl]])

        def prep_bm_kt_mm(kt):
            csl = kt % 4
            E("T", lambda csl=csl: nc.tensor.matmul(
                pb[6][0:4, 0:HF // 32], qm[:, :],
                colred[:, csl * (HF // 32):(csl + 1) * (HF // 32)],
                start=True, stop=True),
              reads=[qmB, crslots[csl]], writes=[pbB[6]])
            bsl = kt % 2
            bv = bmsml[:, bsl * 128:bsl * 128 + HF // 32]
            E("V", lambda bv=bv: nc.vector.tensor_scalar(
                bv, pb[6][0:4, 0:HF // 32], 0.0, None, ALU.is_gt),
              reads=[pbB[6]], writes=[bmsB[bsl]])
            E("G", lambda bv=bv, kt=kt: nc.gpsimd.dma_start(
                out=bm_dram[0:4, kt * (HF // 32):(kt + 1) * (HF // 32)], in_=bv),
              reads=[bmsB[bsl]], writes=[bmdB], sem="bw%d" % bsl)
        NQ32 = cfg.QW // 32

        def prep_mask_q(j):
            kt, q = j // NQ, j % NQ
            ksl = kt % 2
            bmv = BMk[:, ksl * (HF // 32):(ksl + 1) * (HF // 32)]
            if q == 0:
                for g4 in range(4):
                    E("G", lambda g4=g4, kt=kt, bmv=bmv: nc.gpsimd.dma_start(
                        out=bmv[32 * g4:32 * g4 + 32, :],
                        in_=bm_dram[g4:g4 + 1, kt * (HF // 32):(kt + 1) * (HF // 32)]
                        .broadcast_to([32, HF // 32])),
                      reads=[bmdB], writes=[BMkB[ksl]], sem="bma")
            sl = j % 2
            pv = pf[:, sl * cfg.QW:(sl + 1) * cfg.QW]
            mv = mkd[:, sl * cfg.QW:(sl + 1) * cfg.QW]
            E("S", lambda q=q, bmv=bmv, ksl=ksl: nc.scalar.copy(
                EXP[:, :].rearrange("p (n r) -> p n r", r=32),
                bmv[:, q * NQ32:(q + 1) * NQ32]
                .unsqueeze(2).broadcast_to([128, NQ32, 32])),
              reads=[BMkB[ksl]], writes=[EXPB])
            E("G", lambda pv=pv, kt=kt, q=q: nc.gpsimd.dma_start(
                out=pv,
                in_=waT[kt * 128:(kt + 1) * 128, q * cfg.QW:(q + 1) * cfg.QW]),
              writes=[pfB[sl]], sem="pf%d" % sl)
            E("V", lambda pv=pv, mv=mv: nc.vector.tensor_tensor(
                mv, pv, EXP[:, :], ALU.mult),
              reads=[pfB[sl], EXPB], writes=[mkdB[sl]])
            bik, kk = kt // cfg.KB, kt % cfg.KB
            ng_per_q = cfg.QW // cfg.GW
            E("G", lambda mv=mv, bik=bik, kk=kk, q=q, ng_per_q=ng_per_q:
              nc.gpsimd.dma_start(
                out=a_bf[q * ng_per_q:(q + 1) * ng_per_q, bik, :,
                         kk * cfg.GW:(kk + 1) * cfg.GW]
                .rearrange("g p w -> p g w"),
                in_=mv.rearrange("p (g w) -> p g w", w=cfg.GW)),
              reads=[mkdB[sl]], writes=[abfB], sem="ab%d" % sl)

        class Hb:
            def __init__(self, view, alias):
                self.v = view
                self.buf = Buf()
                self.q4 = [Buf() for _ in range(4)]
                self.alias = alias

        htstream = [Buf() for _ in range(NR)]
        hb_HT = Hb(HT[:, :], [])
        hb_IA = Hb(inter[:, 0:ND * C], [interB]) if NI >= ND else None
        hb_IB = Hb(inter[:, ND * C:2 * ND * C], [interB]) if NI >= 2 * ND else None

        def ln_load(c, src_ap, src_regs, r, hb):
            ccols = slice(c * C, (c + 1) * C)
            d0 = r * cfg.BQ
            nb = min(cfg.BQ, ND - d0)
            E("G", lambda d0=d0, nb=nb, hb=hb: nc.gpsimd.dma_start(
                out=hb.v[:, d0 * C:(d0 + nb) * C].rearrange("p (j t) -> p j t", t=C),
                in_=src_ap[d0 * 128:(d0 + nb) * 128, ccols]
                .rearrange("(j p) t -> p j t", p=128)),
              reads=[src_regs[r]], writes=[hb.buf, htstream[r]] + hb.alias,
              sem="ht%d" % r)

        def ln_stats_region(r, hb):
            d0 = r * cfg.BQ
            nb = min(cfg.BQ, ND - d0)
            for dt in range(d0, d0 + nb):
                sl = dt % 2
                sv = silb[:, sl * C:(sl + 1) * C]
                hv = hb.v[:, dt * C:(dt + 1) * C]
                E("V", lambda sv=sv, hv=hv: nc.vector.tensor_tensor(sv, hv, hv, ALU.mult),
                  reads=[hb.buf], writes=[sq[sl]])
                E("T", lambda hv=hv, dt=dt: nc.tensor.matmul(
                    pb[2][0:1, 0:C], oneskb[:, :], hv,
                    start=(dt == 0), stop=(dt == ND - 1)),
                  reads=[oneskB, hb.buf], writes=[pbB[2]], waw_ok=(dt > 0))
                E("T", lambda sv=sv, dt=dt: nc.tensor.matmul(
                    pb[3][0:1, 0:C], oneskb[:, :], sv,
                    start=(dt == 0), stop=(dt == ND - 1)),
                  reads=[oneskB, sq[sl]], writes=[pbB[3]], waw_ok=(dt > 0))

        def ln_finalize():
            inv_d = 1.0 / D
            E("V", lambda: nc.vector.tensor_scalar(mrow[:, :], pb[2][0:1, 0:C],
                                                   inv_d, None, ALU.mult),
              reads=[pbB[2]], writes=[mrowB])
            E("T", lambda: nc.tensor.matmul(pb[2][:, 0:C], onesb1[:, :], mrow[:, :],
                                            start=True, stop=True),
              reads=[onesB, mrowB], writes=[pbB[2]])
            E("S", lambda: nc.scalar.copy(mb_sb[:, :], pb[2][:, 0:C]),
              reads=[pbB[2]], writes=[mbB])
            E("V", lambda: nc.vector.tensor_scalar(srow[:, :], pb[3][0:1, 0:C],
                                                   inv_d, None, ALU.mult),
              reads=[pbB[3]], writes=[srowB])
            E("V", lambda: nc.vector.tensor_tensor(rstd[:, :], mrow[:, :], mrow[:, :],
                                                   ALU.mult),
              reads=[mrowB], writes=[rstdB])
            E("V", lambda: nc.vector.tensor_tensor(srow[:, :], srow[:, :], rstd[:, :],
                                                   ALU.subtract),
              reads=[srowB, rstdB], writes=[srowB])
            E("S", lambda: nc.scalar.activation(rstd[:, :], srow[:, :], AF.Sqrt,
                                                bias=epsv[0:1, 0:1]),
              reads=[srowB, epsvB], writes=[rstdB])
            E("V", lambda: nc.vector.reciprocal(rstd[:, :], rstd[:, :]),
              reads=[rstdB], writes=[rstdB])
            E("T", lambda: nc.tensor.matmul(pb[3][:, 0:C], onesb1[:, :], rstd[:, :],
                                            start=True, stop=True),
              reads=[onesB, rstdB], writes=[pbB[3]])
            E("S", lambda: nc.scalar.copy(rb_sb[:, :], pb[3][:, 0:C]),
              reads=[pbB[3]], writes=[rbB])

        def ln_affine(hb):
            for dt in range(ND):
                hv = hb.v[:, dt * C:(dt + 1) * C]
                E("V", lambda hv=hv: nc.vector.tensor_tensor(hv, hv, mb_sb[:, :],
                                                             ALU.subtract),
                  reads=[mbB, hb.buf], writes=[hb.buf] + hb.alias)
            for dt in range(ND):
                hv = hb.v[:, dt * C:(dt + 1) * C]
                E("V", lambda hv=hv: nc.vector.tensor_tensor(hv, hv, rb_sb[:, :],
                                                             ALU.mult),
                  reads=[rbB, hb.buf],
                  writes=[hb.buf, hb.q4[dt // max(1, ND // 4)]] + hb.alias)

        def ln_full(c, src_ap, src_regs, hb):
            for r in range(NR):
                ln_load(c, src_ap, src_regs, r, hb)
            for r in range(NR):
                ln_stats_region(r, hb)
            ln_finalize()
            ln_affine(hb)

        def emit_ffn(c, src_ap, src_regs, dst_ap, dst_regs, first, hooks, hb,
                     a_hook=None):
            ccols = slice(c * C, (c + 1) * C)
            GUW = 2 * ND * 128
            for i in range(NI):
                sl = i % 3
                guv = gubuf[:, sl * GUW:(sl + 1) * GUW]
                gv = guv[:, 0:ND * 128]
                uv = guv[:, ND * 128:GUW]
                if first:
                    E("G", lambda gv=gv, i=i: nc.gpsimd.dma_start(
                        out=gv, in_=wg5[i]),
                      writes=[gu[sl]], sem="gbf%d" % sl)
                    E("G", lambda uv=uv, i=i: nc.gpsimd.dma_start(
                        out=uv, in_=wu5[i]),
                      writes=[gu[sl]], waw_ok=True, sem="gbf%d" % sl)
                    E("A", lambda guv=guv, i=i: nc.scalar.dma_start(
                        out=gu_bf[i], in_=guv),
                      reads=[gu[sl]], writes=[gbfB], sem="scg%d" % sl)
                else:
                    E("Y", lambda guv=guv, i=i: nc.sync.dma_start(
                        out=guv, in_=gu_bf[i]),
                      reads=[gbfB], writes=[gu[sl]], sem="gb%d" % sl)
                bG = pbB[0] if i % 2 == 0 else pbB[2]
                bU = pbB[1] if i % 2 == 0 else pbB[3]
                pG = pb[0] if i % 2 == 0 else pb[2]
                pU = pb[1] if i % 2 == 0 else pb[3]

                def mmchain(pX, wv, d0, d1):
                    ins = None
                    for dt in range(d0, d1):
                        ins = nc.tensor.matmul(
                            pX[:, 0:C], wv[:, dt * 128:(dt + 1) * 128],
                            hb.v[:, dt * C:(dt + 1) * C],
                            start=(dt == 0), stop=(dt == ND - 1))
                    return ins
                if i < 2 and ND >= 4:
                    qn = ND // 4
                    for j in range(4):
                        E("T", lambda pG=pG, gv=gv, j=j, qn=qn:
                          mmchain(pG, gv, j * qn, (j + 1) * qn),
                          reads=[gu[sl], hb.q4[j]], writes=[bG], waw_ok=(j > 0))
                    for j in range(4):
                        E("T", lambda pU=pU, uv=uv, j=j, qn=qn:
                          mmchain(pU, uv, j * qn, (j + 1) * qn),
                          reads=[gu[sl], hb.q4[j]], writes=[bU], waw_ok=(j > 0))
                else:
                    E("T", lambda pG=pG, gv=gv: mmchain(pG, gv, 0, ND),
                      reads=[gu[sl], hb.buf], writes=[bG])
                    E("T", lambda pU=pU, uv=uv: mmchain(pU, uv, 0, ND),
                      reads=[gu[sl], hb.buf], writes=[bU])
                slp = i % 2
                sv = silb[:, slp * C:(slp + 1) * C]
                E("S", lambda sv=sv, pG=pG: nc.scalar.activation(
                    sv, pG[:, 0:C], AF.Silu, scale=gsv[:, 0:1]),
                  reads=[bG, gsvB], writes=[sil[slp]])
                E("V", lambda sv=sv, pU=pU, i=i: nc.vector.tensor_tensor(
                    inter[:, i * C:(i + 1) * C], sv, pU[:, 0:C], ALU.mult),
                  reads=[sil[slp], bU], writes=[interB])
                if a_hook:
                    a_hook(i)
            if a_hook:
                a_hook(None)
            rot = [(4, 5), (6, 7), (0, 1)]

            def wd_slot(g, bi):
                i0 = bi * cfg.IB
                nb = min(cfg.IB, NI - i0)
                wsl = (g * NBAT + bi) % 3
                wv = wsb[:, wsl * cfg.IB * cfg.GW:wsl * cfg.IB * cfg.GW + nb * cfg.GW]
                return i0, nb, wsl, wv

            def wd_load(g, bi):
                i0, nb, wsl, wv = wd_slot(g, bi)
                if first:
                    E("G", lambda wv=wv, nb=nb, g=g, bi=bi: nc.gpsimd.dma_start(
                        out=wv, in_=wd5[g, bi, :, 0:nb * cfg.GW]),
                      writes=[ws[wsl]], sem="wsf%d" % wsl)
                    E("A", lambda wv=wv, nb=nb, g=g, bi=bi: nc.scalar.dma_start(
                        out=d_bf[g, bi, :, 0:nb * cfg.GW], in_=wv),
                      reads=[ws[wsl]], writes=[dbfB], sem="scd%d" % wsl)
                else:
                    E("Y", lambda wv=wv, nb=nb, g=g, bi=bi: nc.sync.dma_start(
                        out=wv, in_=d_bf[g, bi, :, 0:nb * cfg.GW]),
                      reads=[dbfB], writes=[ws[wsl]], sem="ws%d" % wsl)

            pairs = [(g, bi) for g in range(cfg.NGRP) for bi in range(NBAT)]
            cursor = [0]

            def wd_prefetch(upto):
                while cursor[0] < min(upto, len(pairs)):
                    wd_load(*pairs[cursor[0]])
                    cursor[0] += 1

            kidx = 0
            for g in range(cfg.NGRP):
                k0i, k1i = rot[g % 3]
                xsl = g % 2
                xv = xob[:, xsl * cfg.GN * C:(xsl + 1) * cfg.GN * C]
                E("A", lambda xv=xv, g=g: nc.scalar.dma_start(
                    out=xv.rearrange("p (j t) -> p j t", t=C),
                    in_=src_ap[g * cfg.GW:(g + 1) * cfg.GW, ccols]
                    .rearrange("(j p) t -> p j t", p=128)),
                  reads=[src_regs[g * cfg.GN // cfg.BQ]], writes=[xo[xsl]],
                  sem="xo%d" % xsl)
                for bi in range(NBAT):
                    i0, nb, wsl, wv = wd_slot(g, bi)
                    wv3 = wv.rearrange("p (b w) -> p b w", w=cfg.GW)
                    wd_prefetch(kidx + 2)
                    kidx += 1

                    def bchain(wv3=wv3, i0=i0, nb=nb, k0i=k0i, k1i=k1i):
                        ins = None
                        for ii in range(nb):
                            i = i0 + ii
                            ins = nc.tensor.matmul(
                                pb[k0i][:, 0:C], wv3[:, ii, 0:128],
                                inter[:, i * C:(i + 1) * C],
                                start=(i == 0), stop=(i == NI - 1))
                            ins = nc.tensor.matmul(
                                pb[k1i][:, 0:C], wv3[:, ii, 128:256],
                                inter[:, i * C:(i + 1) * C],
                                start=(i == 0), stop=(i == NI - 1))
                        return ins
                    E("T", bchain, reads=[ws[wsl], interB],
                      writes=[pbB[k0i], pbB[k1i]], waw_ok=(bi > 0))
                for nt in range(cfg.GN):
                    ki = (k0i, k1i)[nt]
                    xvs = xv[:, nt * C:(nt + 1) * C]
                    E("V", lambda xvs=xvs, ki=ki: nc.vector.scalar_tensor_tensor(
                        xvs, pb[ki][:, 0:C], sdv[:, 0:1], xvs, ALU.mult, ALU.add),
                      reads=[pbB[ki], sdvB, xo[xsl]], writes=[xo[xsl]])
                E("A", lambda xv=xv, g=g: nc.scalar.dma_start(
                    out=dst_ap[g * cfg.GW:(g + 1) * cfg.GW, ccols]
                    .rearrange("(j p) t -> p j t", p=128),
                    in_=xv.rearrange("p (j t) -> p j t", t=C)),
                  reads=[xo[xsl]], writes=[dst_regs[g * cfg.GN // cfg.BQ]],
                  sem="xw%d" % xsl)
                hk = hooks.get(g)
                if hk:
                    hk()

        def emit_attn(c, src_ap, src_regs, dst_ap, dst_regs, hooks, hb):
            ccols = slice(c * C, (c + 1) * C)
            rot = [(4, 5), (6, 7), (0, 1)]

            def wa_slot(g, bi):
                k0 = bi * cfg.KB
                nb = min(cfg.KB, ND - k0)
                wsl = (g * NBATK + bi) % 3
                wv = wsb[:, wsl * cfg.IB * cfg.GW:wsl * cfg.IB * cfg.GW + nb * cfg.GW]
                return k0, nb, wsl, wv

            def wa_load(g, bi):
                k0, nb, wsl, wv = wa_slot(g, bi)
                E("Y", lambda wv=wv, nb=nb, g=g, bi=bi: nc.sync.dma_start(
                    out=wv, in_=a_bf[g, bi, :, 0:nb * cfg.GW]),
                  reads=[abfB], writes=[ws[wsl]], sem="ws%d" % wsl)

            pairs = [(g, bi) for g in range(cfg.NGRP) for bi in range(NBATK)]
            cursor = [0]

            def wa_prefetch(upto):
                while cursor[0] < min(upto, len(pairs)):
                    wa_load(*pairs[cursor[0]])
                    cursor[0] += 1

            kidx = 0
            for g in range(cfg.NGRP):
                k0i, k1i = rot[g % 3]
                xsl = g % 2
                xv = xob[:, xsl * cfg.GN * C:(xsl + 1) * cfg.GN * C]
                E("A", lambda xv=xv, g=g: nc.scalar.dma_start(
                    out=xv.rearrange("p (j t) -> p j t", t=C),
                    in_=src_ap[g * cfg.GW:(g + 1) * cfg.GW, ccols]
                    .rearrange("(j p) t -> p j t", p=128)),
                  reads=[src_regs[g * cfg.GN // cfg.BQ]], writes=[xo[xsl]],
                  sem="xo%d" % xsl)
                for bi in range(NBATK):
                    k0, nb, wsl, wv = wa_slot(g, bi)
                    wv3 = wv.rearrange("p (b w) -> p b w", w=cfg.GW)
                    wa_prefetch(kidx + 2)
                    kidx += 1

                    def achain(wv3=wv3, k0=k0, nb=nb, k0i=k0i, k1i=k1i):
                        ins = None
                        for kk in range(nb):
                            kt = k0 + kk
                            ins = nc.tensor.matmul(
                                pb[k0i][:, 0:C], wv3[:, kk, 0:128],
                                hb.v[:, kt * C:(kt + 1) * C],
                                start=(kt == 0), stop=(kt == ND - 1))
                            ins = nc.tensor.matmul(
                                pb[k1i][:, 0:C], wv3[:, kk, 128:256],
                                hb.v[:, kt * C:(kt + 1) * C],
                                start=(kt == 0), stop=(kt == ND - 1))
                        return ins
                    hq = hb.q4[min(3, ((bi + 1) * cfg.KB - 1) // max(1, ND // 4))] \
                        if g == 0 and ND >= 4 else hb.buf
                    E("T", achain, reads=[ws[wsl], hq],
                      writes=[pbB[k0i], pbB[k1i]], waw_ok=(bi > 0))
                for nt in range(cfg.GN):
                    ki = (k0i, k1i)[nt]
                    gnt = g * cfg.GN + nt
                    xvs = xv[:, nt * C:(nt + 1) * C]
                    E("V", lambda xvs=xvs, ki=ki, gnt=gnt: nc.vector.scalar_tensor_tensor(
                        xvs, pb[ki][:, 0:C], biasb[:, gnt:gnt + 1], xvs,
                        ALU.add, ALU.add),
                      reads=[pbB[ki], biasB, xo[xsl]], writes=[xo[xsl]])
                E("A", lambda xv=xv, g=g: nc.scalar.dma_start(
                    out=dst_ap[g * cfg.GW:(g + 1) * cfg.GW, ccols]
                    .rearrange("(j p) t -> p j t", p=128),
                    in_=xv.rearrange("p (j t) -> p j t", t=C)),
                  reads=[xo[xsl]], writes=[dst_regs[g * cfg.GN // cfg.BQ]],
                  sem="xw%d" % xsl)
                hk = hooks.get(g)
                if hk:
                    hk()

        jobs = []
        for c in range(cfg.NCH):
            jobs.append(("ffn", c, xT_in, xin_regs, xT_buf, xbuf_regs[c], c == 0))
        for c in range(cfg.NCH):
            jobs.append(("attn", c, xT_buf, xbuf_regs[c], xT_buf, xbuf_regs[c], False))
        for c in range(cfg.NCH):
            jobs.append(("ffn", c, xT_buf, xbuf_regs[c], outT, out_regs[c], False))

        ln_done = [False] * (len(jobs) + 1)

        h_home = []
        a_seen = 0
        for jb in jobs:
            if jb[0] == "ffn":
                h_home.append(hb_HT)
            else:
                if a_seen == 0 or hb_IA is None:
                    h_home.append(hb_HT)
                elif a_seen % 2 == 1 or hb_IB is None:
                    h_home.append(hb_IA)
                else:
                    h_home.append(hb_IB)
                a_seen += 1

        def make_hooks(k):
            if k + 1 >= len(jobs):
                return {}
            n_kind, n_c, n_src, n_sregs = jobs[k + 1][:4]
            hbn = h_home[k + 1]
            state = {"loads": 0, "stats": 0}

            def hook(g):
                def run():
                    if g == 0:
                        while state["loads"] < NR:
                            ln_load(n_c, n_src, n_sregs, state["loads"], hbn)
                            state["loads"] += 1
                    limit = NR if g >= min(2, cfg.NGRP - 1) else 0
                    while state["stats"] < limit:
                        ln_stats_region(state["stats"], hbn)
                        state["stats"] += 1
                    if g == cfg.NGRP - 1:
                        ln_finalize()
                        ln_affine(hbn)
                        ln_done[k + 1] = True
                return run
            return {g: hook(g) for g in range(cfg.NGRP)}

        NQT = ND * NQ
        last_prep = cfg.NCH - 1
        n_hooks = max(1, (last_prep) * NI)
        QPH = max(1, -(-2 * NQT // n_hooks))
        pstate = {"bmq": 0, "bm2": 0, "maskq": 0}

        def prep_piece():
            if pstate["bmq"] < NQT:
                prep_bm_q(pstate["bmq"]); pstate["bmq"] += 1
                while pstate["bm2"] < ND and \
                        (pstate["bm2"] + 1) * NQ + 2 <= pstate["bmq"]:
                    prep_bm_kt_mm(pstate["bm2"]); pstate["bm2"] += 1
                return True
            if pstate["bm2"] < ND:
                prep_bm_kt_mm(pstate["bm2"]); pstate["bm2"] += 1
                return True
            if pstate["maskq"] < NQT:
                prep_mask_q(pstate["maskq"]); pstate["maskq"] += 1
                return True
            return False

        def prep_hook_for(k):
            if not (1 <= k <= last_prep):
                return None
            final = k == last_prep

            def hook(i):
                if i is None:
                    if final:
                        while prep_piece():
                            pass
                    return
                for _ in range(QPH):
                    if not prep_piece():
                        break
            return hook

        for k, jb in enumerate(jobs):
            kind, c, s_ap, s_regs, d_ap, d_regs, first = jb
            if not ln_done[k]:
                ln_full(c, s_ap, s_regs, h_home[k])
                ln_done[k] = True
            if kind == "ffn":
                emit_ffn(c, s_ap, s_regs, d_ap, d_regs, first, make_hooks(k),
                         h_home[k], a_hook=prep_hook_for(k))
            else:
                emit_attn(c, s_ap, s_regs, d_ap, d_regs, make_hooks(k), h_home[k])

        gq = ["ct", "ab0", "ab1", "bw0", "bw1", "bma", "pf0", "pf1",
              "gbf0", "gbf1", "gbf2", "wsf0", "wsf1", "wsf2"]
        gq += ["ht%d" % r for r in range(NR)]
        for s in gq:
            nc.gpsimd.wait_ge(bld.sems[s], bld.cnt[s])
        yq = ["gb0", "gb1", "gb2", "ws0", "ws1", "ws2"]
        for s in yq:
            nc.sync.wait_ge(bld.sems[s], bld.cnt[s])
        for s in ("scg0", "scg1", "scg2", "scd0", "scd1", "scd2",
                  "xo0", "xo1", "xw0", "xw1"):
            nc.scalar.wait_ge(bld.sems[s], bld.cnt[s])
        for s in bld.cnt:
            assert bld.cnt[s] < 2 ** 31, (s, bld.cnt[s])
    return nc


def _tile_gate(cfg, w):
    return np.ascontiguousarray(
        w.reshape(cfg.NI, 128, cfg.ND, 128).transpose(0, 3, 2, 1)
        .reshape(cfg.NI, 128, cfg.ND * 128))


def _tile_down(cfg, wd):
    nbat = -(-cfg.NI // cfg.IB)
    pad = nbat * cfg.IB * 128
    wdT = wd.T
    buf = np.zeros((pad, cfg.D), dtype=wd.dtype)
    buf[:cfg.INT] = wdT
    return np.ascontiguousarray(
        buf.reshape(nbat, cfg.IB, 128, cfg.NGRP, cfg.GW)
        .transpose(3, 0, 2, 1, 4).reshape(cfg.NGRP, nbat, 128, cfg.IB * cfg.GW))


def _shared_inputs(cfg, wg, wu, wd, wa, mask, bias, gs, us, ds):
    f32 = np.float32
    qmat = np.zeros((128, 4), dtype=f32)
    for r in range(128):
        qmat[r, r // 32] = 1.0
    return dict(
        wg5=_tile_gate(cfg, np.asarray(wg, dtype=f32)),
        wu5=_tile_gate(cfg, np.asarray(wu, dtype=f32)),
        wd5=_tile_down(cfg, np.asarray(wd, dtype=f32)),
        waT=np.ascontiguousarray(np.asarray(wa, dtype=f32).T),
        maskT=np.ascontiguousarray(np.asarray(mask, dtype=np.int32).T),
        bias_pt=np.ascontiguousarray(np.asarray(bias, dtype=f32).reshape(cfg.NT, 128).T),
        gs_in=np.asarray(gs, dtype=f32).reshape(1, 1),
        us_in=np.asarray(us, dtype=f32).reshape(1, 1),
        ds_in=np.asarray(ds, dtype=f32).reshape(1, 1),
        ones1_in=np.ones((1, 128), dtype=f32),
        oneskb_in=np.ones((128, 1), dtype=ml_dtypes.bfloat16),
        qmat_in=qmat,
    )


def _host_layout(cfg, inputs):
    data = np.asarray(inputs["L0_attn_data"], dtype=np.float32)
    X = np.ascontiguousarray(data.reshape(-1, cfg.D).T)
    shared = _shared_inputs(
        cfg, inputs["L0_ffn_gate_w"], inputs["L0_ffn_up_w"],
        inputs["L0_ffn_down_w"], inputs["L0_attn_weight"], inputs["L0_attn_mask"],
        inputs["L0_attn_bias"], inputs["L0_ffn_gate_scale"],
        inputs["L0_ffn_up_scale"], inputs["L0_ffn_down_scale"])
    in_maps = []
    for j in range(N_CORES):
        m = dict(shared)
        m["xT_in"] = np.ascontiguousarray(X[:, j * cfg.TOK:(j + 1) * cfg.TOK])
        in_maps.append(m)
    return in_maps


def kernel(**inputs):
    cfg = REAL
    in_maps = _host_layout(cfg, inputs)
    nc = build_nc(cfg)
    res = run_bass_kernel_spmd(nc, in_maps, core_ids=list(range(N_CORES)))
    shards = [np.asarray(r["outT"]) for r in res.results]
    full = np.concatenate(shards, axis=1)
    out = np.ascontiguousarray(full.T).reshape(np.asarray(inputs["L0_attn_data"]).shape)
    return out.astype(np.float32, copy=False)
